# version 36
# baseline (speedup 1.0000x reference)
import numpy as np
from contextlib import ExitStack

import concourse.bass as bass
import concourse.mybir as mybir
from concourse.bass_utils import run_bass_kernel_spmd
from concourse.alu_op_type import AluOpType

F32 = mybir.dt.float32
BF16 = mybir.dt.bfloat16
AF = mybir.ActivationFunctionType
AX = mybir.AxisListType.X

D = 1024
DC = D // 128
HQ, HKV, HD = 16, 4, 64
DH = 4096
FC = DH // 128
EPS = 1e-5
NEG = -1e30

PAIRS = [(c, c + 4) for c in range(4)] + [(c + 4, c + 8) for c in range(4, 8)]


class Sem:
    def __init__(self, h):
        self.h = h
        self.n = 0

    def inc(self, k=1):
        self.n += k
        return self.n


class Builder:
    ENGS = ("sync", "tensor", "vector", "scalar", "gpsimd")

    def __init__(self, nc):
        self.nc = nc
        self.q = {e: [] for e in self.ENGS}
        self.stack = ExitStack()

    def sem(self, name):
        return Sem(self.stack.enter_context(self.nc.semaphore(name)))

    def emit(self, eng, fn):
        self.q[eng].append(fn)

    def op(self, eng, build, inc=None, inc_n=1):
        if inc is not None:
            inc.inc(inc_n)
            self.emit(eng, lambda e, b=build, s=inc, k=inc_n: b(e).then_inc(s.h, k))
        else:
            self.emit(eng, build)

    def wait(self, eng, sem, val=None):
        v = sem.n if val is None else val
        if v > 0:
            self.emit(eng, lambda e, s=sem, vv=v: e.wait_ge(s.h, vv))

    def dma(self, eng, out_ap, in_ap, sem, **kw):
        sem.inc(16)
        self.emit(
            eng,
            lambda e, o=out_ap, i=in_ap, s=sem, kw=kw: e.dma_start(
                out=o, in_=i, **kw
            ).then_inc(s.h, 16),
        )

    def sbuf(self, name, shape, dtype):
        return self.stack.enter_context(self.nc.sbuf_tensor(name, shape, dtype))

    def psum(self, name, shape, dtype):
        return self.stack.enter_context(self.nc.psum_tensor(name, shape, dtype))


def build_nc(T=2048, ffn_stop=None):
    assert T % 256 == 0
    TB = T // 128
    Tf = T // 2
    TFB = Tf // 128
    NB = T // 512
    SEGf = [(i * 512, min(512, Tf - i * 512)) for i in range((Tf + 511) // 512)]
    nc = bass.Bass("TRN2", target_bir_lowering=False)
    b = Builder(nc)

    x_d = nc.dram_tensor("x", [T, D], F32, kind="ExternalInput").ap()
    wq_d = nc.dram_tensor("wq", [D, HQ * HD], F32, kind="ExternalInput").ap()
    wk_d = nc.dram_tensor("wk", [D, HKV * HD], F32, kind="ExternalInput").ap()
    wv_d = nc.dram_tensor("wv", [D, HKV * HD], F32, kind="ExternalInput").ap()
    wo_d = nc.dram_tensor("wo", [HQ * HD, D], F32, kind="ExternalInput").ap()
    w1_d = nc.dram_tensor("w1", [DH, D], F32, kind="ExternalInput").ap()
    w2_d = nc.dram_tensor("w2", [DH, D], F32, kind="ExternalInput").ap()
    h1_d = nc.dram_tensor("h1", [DH, DH], F32, kind="ExternalInput").ap()
    h2_d = nc.dram_tensor("h2", [DH, DH], F32, kind="ExternalInput").ap()
    w3_d = nc.dram_tensor("w3", [D, DH], F32, kind="ExternalInput").ap()
    s1_d = nc.dram_tensor("s1v", [DH], F32, kind="ExternalInput").ap()
    s2_d = nc.dram_tensor("s2v", [DH], F32, kind="ExternalInput").ap()
    sh1_d = nc.dram_tensor("sh1v", [DH], F32, kind="ExternalInput").ap()
    sh2_d = nc.dram_tensor("sh2v", [DH], F32, kind="ExternalInput").ap()
    s3_d = nc.dram_tensor("s3v", [D], F32, kind="ExternalInput").ap()
    out_d = nc.dram_tensor("out", [T, D], F32, kind="ExternalOutput").ap()
    w1b_d = nc.dram_tensor("w1b", [DH, D], BF16, kind="Internal").ap()
    w2b_d = nc.dram_tensor("w2b", [DH, D], BF16, kind="Internal").ap()
    h1b_d = nc.dram_tensor("h1b", [DH, DH], BF16, kind="Internal").ap()
    h2b_d = nc.dram_tensor("h2b", [DH, DH], BF16, kind="Internal").ap()
    w3b_d = nc.dram_tensor("w3b", [D, DH], BF16, kind="Internal").ap()
    h1t_d = nc.dram_tensor("h1t", [DH, DH], BF16, kind="Internal").ap()
    h2t_d = nc.dram_tensor("h2t", [DH, DH], BF16, kind="Internal").ap()
    w1t_d = nc.dram_tensor("w1t", [D, DH], BF16, kind="Internal").ap()
    w2t_d = nc.dram_tensor("w2t", [D, DH], BF16, kind="Internal").ap()
    w3t_d = nc.dram_tensor("w3t", [DH, D], BF16, kind="Internal").ap()
    lnst_d = nc.dram_tensor("lnst", [T, D], BF16, kind="Internal").ap()
    ln2t_d = nc.dram_tensor("ln2t", [D, T], BF16, kind="Internal").ap()
    x1_d = nc.dram_tensor("x1st", [T, D], F32, kind="Internal").ap()

    PO = {}
    off = 0
    for name, sz in [
        ("q", 8 * T), ("ctx", 8 * T),
        ("lnfm", 8 * T), ("k", 2 * T), ("v", TB * 256),
        ("pbb", 4 * 1024), ("pt", 2 * 1024), ("lnt", 2 * 1024),
        ("wq", 8 * 1024), ("wk", 2 * 1024), ("wv", 2 * 1024), ("wo", 8 * 1024),
    ]:
        PO[name] = off
        off += sz
    GO = 0
    HO = 16 * T
    TPO = HO + FC * Tf
    if DC * Tf + 16384 <= FC * Tf:
        TP12O = HO + DC * Tf
    else:
        TP12O = TPO + 16384
    POOL_N = max(off, TPO + 16384, HO + FC * 1024, TP12O + 16384)
    pool = b.sbuf("pool", [128, POOL_N], BF16)

    def P(name, lo, n):
        return pool[:, PO[name] + lo: PO[name] + lo + n]

    f32pool = b.sbuf("f32pool", [128, 8192], F32)
    xbuf = [f32pool[:, i * 1024:(i + 1) * 1024] for i in range(2)]
    pbf = [f32pool[:, 2048 + i * 1024: 2048 + (i + 1) * 1024] for i in range(4)]
    x1t = [f32pool[:, 6144 + i * 1024: 6144 + (i + 1) * 1024] for i in range(2)]
    x1t.append(f32pool[:, 5120:6144])
    silbuf = f32pool[:, 0:1024]
    w3tmp = f32pool[:, 1024:2048]
    outbuf = [f32pool[:, 2048 + i * 1024: 2048 + (i + 1) * 1024] for i in range(2)]
    w3s_bc = f32pool[:, 4096:5120]

    w3s_row = b.sbuf("w3s_row", [1, 1024], F32)
    sstage = f32pool[0:32, 6144:6656]
    maskT = b.sbuf("maskT", [128, 128], F32)[:, :]
    identb = b.sbuf("identb", [128, 128], BF16)[:, :]
    identf = b.sbuf("identf", [128, 128], F32)[:, :]
    onesf = b.sbuf("onesf", [1, 128], F32)[:, :]
    onesplus = b.sbuf("onesplus", [128, 192], BF16)[:, :]
    s1s = b.sbuf("s1s", [128, FC], F32)
    s2s = b.sbuf("s2s", [128, FC], F32)
    sh1s = b.sbuf("sh1s", [128, FC], F32)
    sh2s = b.sbuf("sh2s", [128, FC], F32)
    stats = b.sbuf("stats", [128, 20], F32)
    asm = b.sbuf("asm", [128, 16], F32)

    ps = [b.psum("ps_a", [128, 1024], F32),
          b.psum("ps_b", [128, 1024], F32),
          b.psum("ps_c", [128, 1024], F32)]
    ps_ctx = b.psum("ps_ctx", [128, 512], F32)
    ps_bc = b.psum("ps_bc", [128, 512], F32)

    sPE = b.sem("sPE")
    sACT = b.sem("sACT")
    sDVE = b.sem("sDVE")
    sGPS = b.sem("sGPS")
    dX = [b.sem(f"dX{i}") for i in range(4)]
    dXr = [b.sem(f"dXr{i}") for i in range(3)]
    dLN = [b.sem(f"dLN{i}") for i in range(2)]
    dLN2 = [b.sem(f"dLN2{i}") for i in range(2)]
    dLNi = b.sem("dLNi")
    dLNc = [b.sem(f"dLNc{i}") for i in range(DC)]
    dW3g = [b.sem(f"dW3g{i}") for i in range(8)]
    dLT = [b.sem(f"dLT{i}") for i in range(2)]
    dW = b.sem("dW")
    dCast = {w: b.sem(f"dC_{w}") for w in ("w1", "w2", "h1", "h2", "w3")}
    dSW = [b.sem(f"dSW{i}") for i in range(2)]
    dSH = [b.sem(f"dSH{i}") for i in range(2)]
    dMX = [b.sem(f"dMX{i}") for i in range(3)]
    dMS = [b.sem(f"dMS{i}") for i in range(3)]
    dW3 = b.sem("dW3")
    dX1 = [b.sem(f"dX1{i}") for i in range(3)]
    dX1r = [b.sem(f"dX1r{i}") for i in range(2)]
    dO = b.sem("dO")

    xb4 = [xbuf[0], xbuf[1], pbf[1], pbf[2]]
    for tb in range(min(4, TB)):
        b.dma("sync", xb4[tb], x_d[tb * 128:(tb + 1) * 128, :], dX[tb % 4])

    b.op("gpsimd", lambda e: e.memset(maskT, 0.0), inc=sGPS)
    b.wait("gpsimd", sGPS)
    b.op("gpsimd", lambda e: e.affine_select(
        maskT, maskT, compare_op=AluOpType.is_ge, fill=NEG, base=0,
        pattern=[[1, 128]], channel_multiplier=-1), inc=sGPS)
    b.op("gpsimd", lambda e: e.memset(onesplus, 1.0), inc=sGPS)
    b.wait("gpsimd", sGPS)
    b.op("gpsimd", lambda e: e.memset(onesplus[:, 64:128], 0.0), inc=sGPS)
    b.op("gpsimd", lambda e: e.memset(
        pool[:, PO["v"]: PO["v"] + TB * 448], 0.0), inc=sGPS)
    b.op("gpsimd", lambda e: e.memset(identb, 0.0), inc=sGPS)
    b.wait("gpsimd", sGPS)
    b.op("gpsimd", lambda e: e.affine_select(
        identb, identb, compare_op=AluOpType.not_equal, fill=1.0, base=0,
        pattern=[[-1, 128]], channel_multiplier=1), inc=sGPS)
    b.op("gpsimd", lambda e: e.memset(identf, 0.0), inc=sGPS)
    b.wait("gpsimd", sGPS)
    b.op("gpsimd", lambda e: e.affine_select(
        identf, identf, compare_op=AluOpType.not_equal, fill=1.0, base=0,
        pattern=[[-1, 128]], channel_multiplier=1), inc=sGPS)
    b.op("gpsimd", lambda e: e.memset(onesf, 1.0), inc=sGPS)
    b.op("gpsimd", lambda e: e.memset(stats[:, 16:17], EPS), inc=sGPS)
    GPS_CONST = sGPS.n

    for ic in range(DC):
        b.dma("gpsimd", P("wq", ic * 1024, 1024), wq_d[ic * 128:(ic + 1) * 128, :], dW)
        b.dma("gpsimd", P("wk", ic * 256, 256), wk_d[ic * 128:(ic + 1) * 128, :], dW)
        b.dma("gpsimd", P("wv", ic * 256, 256), wv_d[ic * 128:(ic + 1) * 128, :], dW)
    for c in range(8):
        ha, hb = PAIRS[c]
        b.dma("gpsimd", P("wo", c * 1024, 1024)[0:64, :],
              wo_d[ha * 64:(ha + 1) * 64, :], dW)
        b.dma("gpsimd", P("wo", c * 1024, 1024)[64:128, :],
              wo_d[hb * 64:(hb + 1) * 64, :], dW)
    for i_, s_dd in enumerate((s1_d, s2_d, sh1_d, sh2_d)):
        b.dma("gpsimd", sstage[:, i_ * 128:(i_ + 1) * 128],
              s_dd.rearrange("(r c) -> r c", r=32), dW)
    b.dma("gpsimd", w3s_row[:, :], s3_d.rearrange("(a o) -> a o", a=1), dW)
    DW_SETUP = dW.n
    b.wait("tensor", dW, DW_SETUP)
    b.wait("tensor", sGPS, GPS_CONST)
    for i_, s_sb in enumerate((s1s, s2s, sh1s, sh2s)):
        b.op("tensor", lambda e, i_=i_: e.transpose(
            ps_ctx[:, i_ * 32:(i_ + 1) * 32],
            sstage[:, i_ * 128:(i_ + 1) * 128], identf[0:32, 0:32]), inc=sPE)
    b.wait("vector", sPE)
    for i_, s_sb in enumerate((s1s, s2s, sh1s, sh2s)):
        b.op("vector", lambda e, i_=i_, o=s_sb: e.tensor_copy(
            o[:, 0:FC], ps_ctx[:, i_ * 32: i_ * 32 + FC]), inc=sDVE)
    SCALES_RDY = sDVE.n

    def cast_w(src, dst, rows, piece, sem):
        for r0 in range(0, rows, piece):
            b.dma("gpsimd", dst[r0:r0 + piece, :], src[r0:r0 + piece, :], sem)

    def emit_ln_head(x_ap, sq_scratch, wait_dve, wait_act, sb):
        def S(c):
            return stats[:, sb + c: sb + c + 1]
        for sem, val in wait_dve:
            b.wait("vector", sem, val)
        for sem, val in wait_act:
            b.wait("scalar", sem, val)
        b.op("vector", lambda e: e.reduce_sum(S(0), x_ap, axis=AX), inc=sDVE)
        b.op("scalar", lambda e: e.activation(
            sq_scratch, x_ap, AF.Square, accum_out=S(1)), inc=sACT)
        sq_done = sACT.n
        b.wait("scalar", sDVE)
        b.op("scalar", lambda e: e.activation(
            S(2), S(0), AF.Copy, scale=1.0 / D), inc=sACT)
        b.wait("vector", sACT)
        b.op("vector", lambda e: e.tensor_scalar(
            S(3), S(2), -1.0, None, AluOpType.mult), inc=sDVE)
        b.wait("vector", sDVE)
        b.op("vector", lambda e: e.scalar_tensor_tensor(
            S(4), S(3), S(2), stats[:, 16:17],
            AluOpType.mult, AluOpType.add), inc=sDVE)
        b.wait("scalar", sDVE)
        b.wait("scalar", sACT, sq_done)
        b.op("scalar", lambda e: e.activation(
            S(5), S(1), AF.Ln, scale=1.0 / D, bias=S(4)), inc=sACT)
        b.wait("scalar", sACT)
        b.op("scalar", lambda e: e.activation(
            S(6), S(5), AF.Exp, scale=-0.5), inc=sACT)
        return sACT.n, sDVE.n

    def emit_ln_tail(x_ap, out_ap, sb, exp_cnt, dve_cnt, wait_out):
        def S(c):
            return stats[:, sb + c: sb + c + 1]
        b.wait("vector", sACT, exp_cnt)
        b.wait("vector", sDVE, dve_cnt)
        for sem, val in wait_out:
            b.wait("vector", sem, val)
        b.op("vector", lambda e: e.tensor_tensor(
            S(7), S(3), S(6), AluOpType.mult), inc=sDVE)
        b.wait("vector", sDVE)
        b.op("vector", lambda e: e.tensor_scalar(
            out_ap, x_ap, S(6), S(7), AluOpType.mult, AluOpType.add),
            inc=sDVE)
        return (sDVE, sDVE.n)

    lnt = [P("lnt", i * 1024, 1024) for i in range(2)]
    b.wait("scalar", sGPS, GPS_CONST)
    b.wait("vector", sGPS, GPS_CONST)
    lnp = {0: (0, 0), 1: (0, 0)}

    def p1_tail(pd_):
        tb_, j_, s_ = pd_["tb"], pd_["j"], pd_["s"]
        nrm = emit_ln_tail(xb4[j_], lnt[s_], 8 * s_, pd_["exp"], pd_["dve"],
                           [(dLN[s_], pd_["lnwar"])])
        b.wait("sync", nrm[0], nrm[1])
        b.dma("sync", lnst_d[tb_ * 128:(tb_ + 1) * 128, :], lnt[s_], dLN[s_])
        if tb_ + 4 < TB:
            b.dma("sync", xb4[j_],
                  x_d[(tb_ + 4) * 128:(tb_ + 5) * 128, :], dX[j_])

    pend1 = None
    for tb in range(TB):
        j = tb % 4
        s = tb % 2
        pa, pd = lnp[s]
        wd = [(dX[j], 16 * (tb // 4 + 1)), (sACT, pa)]
        wa = [(dX[j], 16 * (tb // 4 + 1)), (sDVE, pd)]
        lnwar = dLN[s].n
        exp_cnt, dve_cnt = emit_ln_head(xb4[j], pbf[0], wd, wa, 8 * s)
        lnp[s] = (exp_cnt, dve_cnt)
        cur = {"tb": tb, "j": j, "s": s, "exp": exp_cnt, "dve": dve_cnt,
               "lnwar": lnwar}
        if pend1 is not None:
            p1_tail(pend1)
        pend1 = cur
    p1_tail(pend1)
    p1_dve = sDVE.n
    for sem in dLN:
        b.wait("sync", sem)
    ln1_rdy = []
    for ic in range(DC):
        b.dma("sync", P("lnfm", ic * T, T),
              lnst_d[:, ic * 128:(ic + 1) * 128], dLNc[ic], transpose=True)
        ln1_rdy.append(dLNc[ic].n)
    for ic in range(DC):
        b.wait("gpsimd", dLNc[ic], ln1_rdy[ic])
    cast_w(w1_d, w1b_d, DH, 1024, dCast["w1"])
    cast_w(w2_d, w2b_d, DH, 1024, dCast["w2"])
    cast_w(w3_d, w3b_d, D, 512, dCast["w3"])
    cast_w(h1_d, h1b_d, DH, 512, dCast["h1"])
    cast_w(h2_d, h2b_d, DH, 512, dCast["h2"])

    psum_half = []
    for t_ in ps:
        psum_half.append(t_[:, 0:512])
        psum_half.append(t_[:, 512:1024])
    slot_free = [0] * 6
    si = 0

    b.wait("tensor", dW, DW_SETUP)

    for c in range(8):
        ha, hb = PAIRS[c]
        for nb in range(NB):
            slot = psum_half[si]
            b.wait("tensor", sACT, slot_free[si])
            for idx, h in ((0, ha), (1, hb)):
                for i in range(DC):
                    if c == 0 and nb == 0 and idx == 0:
                        b.wait("tensor", dLNc[i], ln1_rdy[i])
                    b.op("tensor", lambda e, l=P("wq", i * 1024 + h * 64, 64),
                         r=P("lnfm", i * T + nb * 512, 512),
                         o=slot[idx * 64:(idx + 1) * 64, 0:512],
                         st=(i == 0), sp=(i == DC - 1), tp=(0, idx * 64):
                         e.matmul(o, l, r, start=st, stop=sp, tile_position=tp),
                         inc=sPE)
            b.wait("scalar", sPE)
            b.op("scalar", lambda e, o=P("q", c * T + nb * 512, 512), s=slot:
                 e.activation(o, s, AF.Copy, scale=0.125), inc=sACT)
            slot_free[si] = sACT.n
            si = (si + 1) % 6

    def proj(lhsT_list, rhs_list, out_to, n_cols, out_multi=None):
        nonlocal si
        slot = psum_half[si]
        b.wait("tensor", sACT, slot_free[si])
        for i, (l, r) in enumerate(zip(lhsT_list, rhs_list)):
            b.op("tensor", lambda e, l=l, r=r, o=slot[:, 0:n_cols],
                 st=(i == 0), sp=(i == len(lhsT_list) - 1):
                 e.matmul(o, l, r, start=st, stop=sp), inc=sPE)
        b.wait("scalar", sPE)
        if out_multi is not None:
            for dst, plo, w in out_multi:
                b.op("scalar", lambda e, o=dst, s=slot[:, plo:plo + w]:
                     e.activation(o, s, AF.Copy), inc=sACT)
        else:
            b.op("scalar", lambda e, o=out_to, s=slot[:, 0:n_cols]:
                 e.activation(o, s, AF.Copy), inc=sACT)
        slot_free[si] = sACT.n
        si = (si + 1) % 6

    for kc in range(2):
        for nb in range(NB):
            proj([P("wk", ic * 256 + kc * 128, 128) for ic in range(DC)],
                 [P("lnfm", ic * T + nb * 512, 512) for ic in range(DC)],
                 P("k", kc * T + nb * 512, 512), 512)
    VDATA = {0: 0, 1: 256, 2: 128, 3: 384}
    for tb in range(TB):
        proj([P("lnfm", ic * T + tb * 128, 128) for ic in range(DC)],
             [P("wv", ic * 256, 256) for ic in range(DC)],
             None, 256,
             out_multi=[
                 (pool[:, PO["v"] + tb * 448 + VDATA[kv]:
                       PO["v"] + tb * 448 + VDATA[kv] + 64], kv * 64, 64)
                 for kv in range(4)])
    qkv_pe = sPE.n

    b.wait("gpsimd", sDVE, p1_dve)
    for sem in dX:
        b.wait("gpsimd", sem)
    b.dma("gpsimd", xbuf[0], x_d[0:128, :], dXr[0])
    b.dma("gpsimd", xbuf[1], x_d[128:256, :], dXr[1])

    mstage = [P("wq", i * 4096, 4096) for i in range(2)]
    mstage.append(pool[:, PO["wk"]: PO["wk"] + 4096])
    b.wait("sync", sPE)
    MS_done = {}
    u = 0
    for wname, wb, wt, nic, rr in (
            ("w1", w1b_d, w1t_d, DC, DH), ("w2", w2b_d, w2t_d, DC, DH),
            ("w3", w3b_d, w3t_d, FC, D),
            ("h1", h1b_d, h1t_d, FC, DH), ("h2", h2b_d, h2t_d, FC, DH)):
        b.wait("sync", dCast[wname])
        for ic in range(nic):
            par = u % 3
            if u >= 3:
                b.wait("sync", dMS[par], dMS[par].n)
            b.dma("sync", mstage[par][:, 0:rr], wb[:, ic * 128:(ic + 1) * 128],
                  dMX[par], transpose=True)
            b.wait("sync", dMX[par], dMX[par].n)
            b.dma("sync", wt[ic * 128:(ic + 1) * 128, :], mstage[par][:, 0:rr],
                  dMS[par])
            u += 1
        MS_done[wname] = (dMS[0].n, dMS[1].n, dMS[2].n)
    MS_ALL = (dMS[0].n, dMS[1].n, dMS[2].n)

    QGN = T // 512
    pt_strip = [P("pt", i * 512, 512) for i in range(4)]
    sA = [ps[0][:, 0:512], ps[0][:, 512:1024]]
    sB = [ps[1][:, 0:512], ps[1][:, 512:1024]]
    ctxb = [ps[2][:, 0:512], ps[2][:, 512:1024]]
    rsb = [ps_ctx, ps_bc]
    bcast_sb = w3s_bc[:, 0:512]

    PVOFF = {0: 0, 2: 128, 1: 192, 3: 320}

    def Pv(tb_, off_, w_):
        return pool[:, PO["v"] + tb_ * 448 + off_: PO["v"] + tb_ * 448 + off_ + w_]

    b.wait("tensor", sDVE, SCALES_RDY)
    b.wait("tensor", sACT)
    s_free = [[0, 0], [0, 0]]
    pt_free = [0, 0, 0, 0]
    ctx_free = [0, 0]
    rs_free = [0, 0]
    it = 0
    ctx_evac = sDVE.n

    for qbg in range(QGN):
        for c in range(8):
            ha, hb = PAIRS[c]
            units = [(0, ha), (1, hb)]
            kmax = qbg * 4 + 4
            sdone = {}
            mdone = {}
            edone = {}
            for kbp in range(kmax // 2):
                kbs = (2 * kbp, 2 * kbp + 1)
                for kb in kbs:
                    par = kb % 2
                    for u, (idx, h) in enumerate(units):
                        sbank = (sA if u == 0 else sB)[par]
                        base = 64 * idx
                        kv = h // 4
                        kchunk, kbase = kv // 2, 64 * (kv % 2)
                        b.wait("tensor", sACT, s_free[u][par])
                        b.op("tensor", lambda e, o=sbank,
                             l=P("k", kchunk * T + kb * 128, 128)[kbase:kbase + 64, :],
                             rr=P("q", c * T + qbg * 512, 512)[base:base + 64, :],
                             tp=(base, 0):
                             e.matmul(o, l, rr, start=True, stop=True,
                                      tile_position=tp), inc=sPE)
                        sdone[(u, kb)] = sPE.n
                for kb in kbs:
                    par = kb % 2
                    r = kb - qbg * 4
                    w0 = r * 128 if r >= 0 else 0
                    if r >= 0:
                        for u in range(2):
                            sbank = (sA if u == 0 else sB)[par]
                            b.wait("vector", sPE, sdone[(u, kb)])
                            b.op("vector", lambda e,
                                 s=sbank[:, w0:w0 + 128]:
                                 e.tensor_tensor(s, s, maskT, AluOpType.add),
                                 inc=sDVE)
                            mdone[(u, kb)] = sDVE.n
                    for u in range(2):
                        sbank = (sA if u == 0 else sB)[par]
                        strip = pt_strip[u * 2 + par]
                        b.wait("scalar", sPE, sdone[(u, kb)])
                        if r >= 0:
                            b.wait("scalar", sDVE, mdone[(u, kb)])
                        b.wait("scalar", sPE, pt_free[u * 2 + par])
                        if w0 > 0:
                            b.op("scalar", lambda e, o=strip[:, 0:w0],
                                 s=sbank[:, 0:w0]:
                                 e.activation(o, s, AF.Copy, scale=0.0),
                                 inc=sACT)
                        b.op("scalar", lambda e, o=strip[:, w0:512],
                             s=sbank[:, w0:512]:
                             e.activation(o, s, AF.Exp), inc=sACT)
                        s_free[u][par] = sACT.n
                        edone[(u, kb)] = sACT.n
                for kb in kbs:
                    par = kb % 2
                    for u, (idx, h) in enumerate(units):
                        kv = h // 4
                        strip = pt_strip[u * 2 + par]
                        b.wait("tensor", sACT, edone[(u, kb)])
                        if kb == 0 and u == 0:
                            b.wait("tensor", sDVE, ctx_free[it % 2])
                            b.wait("tensor", sDVE, rs_free[it % 2])
                        st_ = (kb == 0 and u == 0)
                        sp_ = (kb == kmax - 1 and u == 1)
                        b.op("tensor", lambda e, o=ctxb[it % 2],
                             l=Pv(kb, PVOFF[kv], 128), rr=strip,
                             st_=st_, sp_=sp_:
                             e.matmul(o, l, rr, start=st_, stop=sp_), inc=sPE)
                        b.op("tensor", lambda e, o=rsb[it % 2][:, :],
                             l=onesplus[:, 64 * u: 64 * u + 128], rr=strip,
                             st_=st_, sp_=sp_:
                             e.matmul(o, l, rr, start=st_, stop=sp_), inc=sPE)
                        pt_free[u * 2 + par] = sPE.n
            pv_done = sPE.n
            b.wait("vector", sPE, pv_done)
            b.op("vector", lambda e, i=rsb[it % 2][:, :]:
                 e.reciprocal(bcast_sb, i), inc=sDVE)
            rs_free[it % 2] = sDVE.n
            b.op("vector", lambda e, o=P("ctx", c * T + qbg * 512, 512),
                 i=ctxb[it % 2]: e.tensor_tensor(
                     o, i, bcast_sb, AluOpType.mult), inc=sDVE)
            ctx_free[it % 2] = sDVE.n
            ctx_evac = sDVE.n
            it += 1
    attn_pe = sPE.n

    b.wait("tensor", dW, DW_SETUP)
    b.wait("tensor", sACT)
    b.wait("tensor", sDVE)
    for nh in range(2):
        b.op("tensor", lambda e, nh=nh: e.matmul(
            ps[2][:, nh * 512:(nh + 1) * 512], onesf,
            w3s_row[:, nh * 512:(nh + 1) * 512], start=True, stop=True),
            inc=sPE)
    bcast_mm = sPE.n
    pre_dve = sDVE.n
    b.wait("vector", sPE, bcast_mm)
    b.wait("vector", sACT)
    b.wait("vector", sGPS)
    b.op("vector", lambda e: e.tensor_copy(w3s_bc, ps[2][:, 0:1024]), inc=sDVE)
    W3SBC = sDVE.n

    tp12 = [pool[:, TP12O + i * 8192: TP12O + (i + 1) * 8192] for i in range(2)]
    gran12_free = [(None, 0), (None, 0)]
    NG12 = DH // 512
    w1t3 = w1t_d.rearrange("(ib p) o -> p ib o", p=128)
    w2t3 = w2t_d.rearrange("(ib p) o -> p ib o", p=128)
    w3t3 = w3t_d.rearrange("(ib p) o -> p ib o", p=128)
    ln2t3 = ln2t_d.rearrange("(ib p) o -> p ib o", p=128)
    w3_last = {}

    W12_HOIST = TP12O == HO + DC * Tf and 4 * T + 8192 <= 8 * T

    def w12_loads(g, w12_first, g12cnt):
        par = g % 2
        gate = MS_done["w2"] if W12_HOIST else MS_ALL
        for q_ in range(3):
            b.wait("gpsimd", dMS[q_], gate[q_])
        eng, cnt = gran12_free[par]
        if eng is not None:
            b.wait("gpsimd", eng, cnt)
        if g in w12_first:
            b.wait("gpsimd", sPE, w12_first[g])
        b.dma("gpsimd",
              tp12[par][:, 0:4096].rearrange("p (i o) -> p i o", o=512),
              w1t3[:, :, g * 512:(g + 1) * 512], dSW[par])
        b.dma("gpsimd",
              tp12[par][:, 4096:8192].rearrange("p (i o) -> p i o", o=512),
              w2t3[:, :, g * 512:(g + 1) * 512], dSW[par])
        g12cnt[g] = dSW[par].n

    g12cnt_p1 = {}
    if W12_HOIST:
        w12_loads(0, {0: qkv_pe, 1: attn_pe}, g12cnt_p1)
        w12_loads(1, {0: qkv_pe, 1: attn_pe}, g12cnt_p1)

    wo_slot_free = [pre_dve, pre_dve, sDVE.n]
    x1_free = [(sACT, sACT.n, 0)] * 3
    lnp4 = {0: (sACT.n, sDVE.n), 1: (sACT.n, sDVE.n)}
    P3_GPS = sGPS.n
    pend4 = None
    TR_P1 = W12_HOIST and Tf == 1024
    psTb = [ps_ctx.bitcast(BF16), ps_bc.bitcast(BF16)]
    psT_war = [0, 0]
    lnt_tr_war = {0: 0, 1: 0}
    ln2cnt = {}

    def p4_tail(pd_):
        tb_, jx_, j2_ = pd_["tb"], pd_["jx"], pd_["j2"]
        nrm = emit_ln_tail(x1t[jx_], lnt[j2_], 8 * j2_, pd_["exp"],
                           pd_["dve"], [(dLN2[j2_], pd_["lnwar"]),
                                        (sPE, lnt_tr_war[j2_])])
        x1_free[jx_] = (nrm[0], nrm[1], pd_["dx1"])
        b.wait("gpsimd", nrm[0], nrm[1])
        b.dma("gpsimd", lnst_d[tb_ * 128:(tb_ + 1) * 128, :], lnt[j2_],
              dLN2[j2_])
        if TR_P1 and tb_ < TFB:
            pT = psTb[tb_ % 2]
            b.wait("tensor", nrm[0], nrm[1])
            b.wait("tensor", sACT, psT_war[tb_ % 2])
            for ic in range(DC):
                b.op("tensor", lambda e, o=pT[:, ic * 128:(ic + 1) * 128],
                     i=lnt[j2_][:, ic * 128:(ic + 1) * 128]:
                     e.transpose(o, i, identb), inc=sPE)
            lnt_tr_war[j2_] = sPE.n
            b.wait("scalar", sPE, lnt_tr_war[j2_])
            for ic in range(DC):
                b.op("scalar", lambda e,
                     o=pool[:, HO + ic * Tf + tb_ * 128:
                            HO + ic * Tf + tb_ * 128 + 128],
                     i=pT[:, ic * 128:(ic + 1) * 128]:
                     e.copy(o, i), inc=sACT)
                ln2cnt[(tb_, ic)] = sACT.n
            psT_war[tb_ % 2] = sACT.n

    b.wait("vector", dW, DW_SETUP)
    for tb in range(TB):
        j3, j2, jx = tb % 2, tb % 2, tb % 3
        slot = ps[tb % 3]
        b.wait("tensor", sDVE, ctx_evac)
        b.wait("tensor", sDVE, wo_slot_free[tb % 3])
        for nh in range(2):
            for i in range(DC):
                b.op("tensor", lambda e, o=slot[:, nh * 512:(nh + 1) * 512],
                     l=P("ctx", i * T + tb * 128, 128),
                     r=P("wo", i * 1024 + nh * 512, 512),
                     st=(i == 0), sp=(i == DC - 1):
                     e.matmul(o, l, r, start=st, stop=sp), inc=sPE)
        wo_done = sPE.n
        b.wait("vector", sPE, wo_done)
        b.wait("vector", dXr[j3], 16 * (tb // 2 + 1))
        lnsem, lnr, dmr = x1_free[jx]
        b.wait("vector", lnsem, lnr)
        b.wait("vector", dX1[jx], dmr)
        if tb == 2:
            b.wait("vector", sGPS, P3_GPS)
        b.op("vector", lambda e, o=x1t[jx], p=slot[:, 0:1024], x=xbuf[j3]:
             e.tensor_tensor(o, p, x, AluOpType.add), inc=sDVE)
        wo_slot_free[tb % 3] = sDVE.n
        x1_done = sDVE.n
        b.wait("gpsimd", sDVE, x1_done)
        b.dma("gpsimd", x1_d[tb * 128:(tb + 1) * 128, :], x1t[jx], dX1[jx])
        pa, pd = lnp4[j2]
        wd = [(sDVE, x1_done), (sACT, pa)]
        wa = [(sDVE, x1_done), (sDVE, pd), (sGPS, P3_GPS)]
        lnwar = dLN2[j2].n
        exp_cnt, dve_cnt = emit_ln_head(x1t[jx], pbf[0], wd, wa, 8 * j2)
        lnp4[j2] = (exp_cnt, dve_cnt)
        cur = {"tb": tb, "jx": jx, "j2": j2, "exp": exp_cnt, "dve": dve_cnt,
               "lnwar": lnwar, "dx1": dX1[jx].n}
        if pend4 is not None:
            p4_tail(pend4)
        pend4 = cur
        if tb + 2 < TB:
            b.wait("gpsimd", sDVE, x1_done)
            b.dma("gpsimd", xbuf[j3], x_d[(tb + 2) * 128:(tb + 3) * 128, :], dXr[j3])
    p4_tail(pend4)

    p4_pe = sPE.n
    tp_buf = [pool[:, TPO + i * 8192: TPO + (i + 1) * 8192] for i in range(2)]
    w3T = pool[:, HO: HO + FC * 1024]
    LN2HO = HO

    gran_free = [(None, 0), (None, 0)]
    pf = {0: (sDVE, sDVE.n), 1: (sDVE, sDVE.n), 2: (sDVE, sDVE.n)}

    for p_ in range(2):
        toff = p_ * Tf
        ln2h_rdy = []
        if p_ == 0 and TR_P1:
            for ic in range(DC):
                ln2h_rdy.append([(sACT, ln2cnt[(3, ic)]),
                                 (sACT, ln2cnt[(7, ic)])])
        elif p_ == 1 and W12_HOIST:
            for q_ in range(2):
                b.wait("gpsimd", dMS[q_], LT_MS[q_])
            for hf in range(2):
                b.wait("gpsimd", sPE, w3_last[4 * hf + 3])
                b.dma("gpsimd",
                      pool[:, LN2HO + hf * 4 * Tf: LN2HO + (hf + 1) * 4 * Tf]
                      .rearrange("p (i o) -> p i o", o=Tf),
                      ln2t3[:, hf * 4:(hf + 1) * 4, toff:toff + Tf], dLT[hf])
                for _ in range(4):
                    ln2h_rdy.append([(dLT[hf], dLT[hf].n)])
        else:
            if p_ == 0 and W12_HOIST:
                ntile = TB // 2
                b.wait("sync", dLN2[0], 16 * ((ntile + 1) // 2))
                b.wait("sync", dLN2[1], 16 * (ntile // 2))
                b.wait("sync", sPE, qkv_pe)
            else:
                b.wait("sync", dLN2[0])
                b.wait("sync", dLN2[1])
                b.wait("sync", sPE)
            for ic in range(DC):
                b.dma("sync", pool[:, LN2HO + ic * Tf: LN2HO + (ic + 1) * Tf],
                      lnst_d[toff:toff + Tf, ic * 128:(ic + 1) * 128],
                      dLNc[ic], transpose=True)
                ln2h_rdy.append([(dLNc[ic], dLNc[ic].n)])
        if p_ == 0 and W12_HOIST:
            x1b = [x1t[0].bitcast(BF16), x1t[1].bitcast(BF16)]
            for sem in dX1:
                b.wait("sync", sem)
            b.wait("sync", sDVE)
            b.wait("sync", sACT)
            lt_ms = {}
            for ic in range(DC):
                par = ic % 2
                if ic >= 2:
                    b.wait("sync", dMS[par], lt_ms[ic - 2])
                b.dma("sync", x1b[par], lnst_d[:, ic * 128:(ic + 1) * 128],
                      dMX[par], transpose=True)
                b.wait("sync", dMX[par], dMX[par].n)
                b.dma("sync", ln2t_d[ic * 128:(ic + 1) * 128, :], x1b[par],
                      dMS[par])
                lt_ms[ic] = dMS[par].n
            LT_MS = (dMS[0].n, dMS[1].n)
        if p_ == 0 and W12_HOIST:
            g12cnt = g12cnt_p1
        else:
            g12cnt = {}
            if p_ == 1 and W12_HOIST:
                first = {0: w3_last[15], 1: w3_last[23]}
            else:
                cur = sPE.n
                first = {0: cur, 1: cur}
            w12_loads(0, first, g12cnt)
            w12_loads(1, first, g12cnt)
        for g in range(NG12):
            par = g % 2
            for oc4 in range(4):
                oc = g * 4 + oc4
                b.wait("tensor", dSW[par], g12cnt[g])
                b.wait("tensor", pf[0][0], pf[0][1])
                for sseg, (so, sw) in enumerate(SEGf):
                    for i in range(DC):
                        if g == 0 and oc4 == 0 and sseg < len(ln2h_rdy[i]):
                            sem_, val_ = ln2h_rdy[i][sseg]
                            b.wait("tensor", sem_, val_)
                        b.op("tensor", lambda e,
                             o=ps[0][:, so:so + sw],
                             l=tp12[par][:, i * 512 + oc4 * 128: i * 512 + oc4 * 128 + 128],
                             r=pool[:, LN2HO + i * Tf + so: LN2HO + i * Tf + so + sw],
                             st=(i == 0), sp=(i == DC - 1):
                             e.matmul(o, l, r, start=st, stop=sp), inc=sPE)
                a_done = sPE.n
                b.wait("tensor", pf[1][0], pf[1][1])
                for so, sw in SEGf:
                    for i in range(DC):
                        b.op("tensor", lambda e,
                             o=ps[1][:, so:so + sw],
                             l=tp12[par][:, 4096 + i * 512 + oc4 * 128: 4096 + i * 512 + oc4 * 128 + 128],
                             r=pool[:, LN2HO + i * Tf + so: LN2HO + i * Tf + so + sw],
                             st=(i == 0), sp=(i == DC - 1):
                             e.matmul(o, l, r, start=st, stop=sp), inc=sPE)
                b_done = sPE.n
                b.wait("scalar", sPE, a_done)
                b.wait("scalar", sDVE)
                b.op("scalar", lambda e, oc=oc, s=ps[0][:, 0:Tf]:
                     e.activation(silbuf[:, 0:Tf], s, AF.Silu,
                                  scale=s1s[:, oc:oc + 1]), inc=sACT)
                pf[0] = (sACT, sACT.n)
                if oc4 == 0 and g + 1 < NG12 and (g + 1) not in g12cnt:
                    w12_loads(g + 1, {}, g12cnt)
                b.wait("vector", sPE, b_done)
                b.wait("vector", sACT)
                b.op("vector", lambda e, oc=oc,
                     o=pool[:, GO + oc * Tf: GO + (oc + 1) * Tf],
                     bb=ps[1][:, 0:Tf]:
                     e.scalar_tensor_tensor(o, bb, s2s[:, oc:oc + 1],
                                            silbuf[:, 0:Tf],
                                            AluOpType.mult, AluOpType.mult),
                     inc=sDVE)
                pf[1] = (sDVE, sDVE.n)
            gran12_free[par] = (sPE, sPE.n)
        g_ready = (sDVE, sDVE.n)
        if ffn_stop == ("g" if p_ == 0 else "g2"):
            b.wait("sync", g_ready[0], g_ready[1])
            break

        def hidden_layer(src_off, dst_off, wtd, ms_done, s_sb, src_ready):
            NGH = DH // 256
            hcnt = {}
            wt3 = wtd.rearrange("(ib p) o -> p ib o", p=128)

            def h_loads(g):
                par = g % 2
                for q_ in range(3):
                    b.wait("gpsimd", dMS[q_], MS_ALL[q_])
                eng, cnt = gran_free[par]
                if eng is None:
                    b.wait("gpsimd", dLN2[0])
                    b.wait("gpsimd", dLN2[1])
                    b.wait("gpsimd", sPE, p4_pe)
                else:
                    b.wait("gpsimd", eng, cnt)
                for bb in range(4):
                    b.dma("gpsimd",
                          tp_buf[par][:, bb * 2048:(bb + 1) * 2048].rearrange(
                              "p (i o) -> p i o", o=256),
                          wt3[:, bb * 8:(bb + 1) * 8, g * 256:(g + 1) * 256],
                          dSH[par])
                hcnt[g] = dSH[par].n

            h_loads(0)
            h_loads(1)
            for g in range(NGH):
                par = g % 2
                for oc2 in range(2):
                    oc = g * 2 + oc2
                    psl = oc % 2
                    pslot = ps[psl]
                    b.wait("tensor", dSH[par], hcnt[g])
                    b.wait("tensor", pf[psl][0], pf[psl][1])
                    b.wait("tensor", src_ready[0], src_ready[1])
                    for so, sw in SEGf:
                        for i in range(FC):
                            b.op("tensor", lambda e,
                                 o=pslot[:, so:so + sw],
                                 l=tp_buf[par][:, i * 256 + oc2 * 128: i * 256 + oc2 * 128 + 128],
                                 r=pool[:, src_off + i * Tf + so: src_off + i * Tf + so + sw],
                                 st=(i == 0), sp=(i == FC - 1):
                                 e.matmul(o, l, r, start=st, stop=sp), inc=sPE)
                    mm_done = sPE.n
                    b.wait("scalar", sPE, mm_done)
                    b.op("scalar", lambda e, oc=oc,
                         o=pool[:, dst_off + oc * Tf: dst_off + (oc + 1) * Tf],
                         s=pslot[:, 0:Tf]:
                         e.activation(o, s, AF.Gelu, scale=s_sb[:, oc:oc + 1]),
                         inc=sACT)
                    pf[psl] = (sACT, sACT.n)
                gran_free[par] = (sPE, sPE.n)
                if g + 2 < NGH:
                    h_loads(g + 2)

        hidden_layer(GO, HO, h1t_d, MS_done["h1"], sh1s, g_ready)
        h1_ready = (sACT, sACT.n)
        if ffn_stop == ("h1" if p_ == 0 else "h12"):
            b.wait("sync", h1_ready[0], h1_ready[1])
            break
        hidden_layer(HO, GO, h2t_d, MS_done["h2"], sh2s, h1_ready)
        h2_ready = (sACT, sACT.n)
        if ffn_stop == ("h2" if p_ == 0 else "h22"):
            b.wait("sync", h2_ready[0], h2_ready[1])
            break

        b.wait("gpsimd", dLN2[0])
        b.wait("gpsimd", dLN2[1])
        for q_ in range(3):
            b.wait("gpsimd", dMS[q_], MS_ALL[q_])
        b.wait("gpsimd", sPE)
        w3g_rdy = []
        for gq in range(8):
            b.dma("gpsimd",
                  w3T[:, gq * 4096:(gq + 1) * 4096].rearrange(
                      "p (i o) -> p i o", o=1024),
                  w3t3[:, gq * 4:(gq + 1) * 4, :], dW3g[gq])
            w3g_rdy.append(dW3g[gq].n)

        for sem in dX1:
            b.wait("sync", sem)
        if p_ == 0 and W12_HOIST:
            b.wait("sync", dMX[0])
            b.wait("sync", dMX[1])
            for q_ in range(2):
                b.wait("sync", dMS[q_], LT_MS[q_])
        for tb in range(TFB):
            j2 = tb % 2
            gtb = p_ * TFB + tb
            b.wait("sync", sGPS)
            b.dma("sync", x1t[j2], x1_d[gtb * 128:(gtb + 1) * 128, :], dX1r[j2])
            b.wait("tensor", pf[2][0], pf[2][1])
            b.wait("tensor", h2_ready[0], h2_ready[1])
            for nh in range(2):
                for i in range(FC):
                    if tb == 0 and nh == 0 and i % 4 == 0:
                        b.wait("tensor", dW3g[i // 4], w3g_rdy[i // 4])
                    b.op("tensor", lambda e,
                         o=ps[2][:, nh * 512:(nh + 1) * 512],
                         l=pool[:, GO + i * Tf + tb * 128: GO + i * Tf + tb * 128 + 128],
                         r=w3T[:, i * 1024 + nh * 512: i * 1024 + nh * 512 + 512],
                         st=(i == 0), sp=(i == FC - 1):
                         e.matmul(o, l, r, start=st, stop=sp), inc=sPE)
                    if p_ == 0 and tb == TFB - 1 and nh == 1:
                        w3_last[i] = sPE.n
            w3_done = sPE.n
            b.wait("vector", sPE, w3_done)
            b.wait("vector", sDVE, W3SBC)
            b.wait("vector", sGPS)
            b.op("vector", lambda e: e.tensor_tensor(
                w3tmp, ps[2][:, 0:1024], w3s_bc, AluOpType.mult), inc=sDVE)
            pf[2] = (sDVE, sDVE.n)
            b.wait("gpsimd", sDVE)
            b.wait("gpsimd", dX1r[j2], 16 * (gtb // 2 + 1))
            b.wait("gpsimd", dO, dO.n)
            b.op("gpsimd", lambda e, o=outbuf[j2], x1_=x1t[j2]:
                 e.tensor_add(o, w3tmp, x1_), inc=sGPS)
            b.wait("sync", sGPS)
            b.dma("sync", out_d[gtb * 128:(gtb + 1) * 128, :], outbuf[j2], dO)

    b.wait("sync", dO)
    b.wait("gpsimd", dO)

    with b.stack:
        with nc.Block() as block:
            @block.sync
            def _(e):
                for f in b.q["sync"]:
                    f(e)

            @block.tensor
            def _(e):
                for f in b.q["tensor"]:
                    f(e)

            @block.vector
            def _(e):
                for f in b.q["vector"]:
                    f(e)

            @block.scalar
            def _(e):
                for f in b.q["scalar"]:
                    f(e)

            @block.gpsimd
            def _(e):
                for f in b.q["gpsimd"]:
                    f(e)

    nc.finalize()
    return nc


def s_slot_free_prev_read(sl, s_slot_free):
    return s_slot_free[sl]


def kernel(L0_attn_x, L0_attn_Wq, L0_attn_Wk, L0_attn_Wv, L0_attn_Wo,
           L0_ffn_w1_w, L0_ffn_w1_s, L0_ffn_w2_w, L0_ffn_w2_s,
           L0_ffn_h1_w, L0_ffn_h1_s, L0_ffn_h2_w, L0_ffn_h2_s,
           L0_ffn_w3_w, L0_ffn_w3_s):
    n = 8
    x = np.ascontiguousarray(np.asarray(L0_attn_x, dtype=np.float32))
    shared = {
        "wq": np.ascontiguousarray(np.asarray(L0_attn_Wq, np.float32)),
        "wk": np.ascontiguousarray(np.asarray(L0_attn_Wk, np.float32)),
        "wv": np.ascontiguousarray(np.asarray(L0_attn_Wv, np.float32)),
        "wo": np.ascontiguousarray(np.asarray(L0_attn_Wo, np.float32)),
        "w1": np.ascontiguousarray(np.asarray(L0_ffn_w1_w, np.float32)),
        "w2": np.ascontiguousarray(np.asarray(L0_ffn_w2_w, np.float32)),
        "h1": np.ascontiguousarray(np.asarray(L0_ffn_h1_w, np.float32)),
        "h2": np.ascontiguousarray(np.asarray(L0_ffn_h2_w, np.float32)),
        "w3": np.ascontiguousarray(np.asarray(L0_ffn_w3_w, np.float32)),
        "s1v": np.ascontiguousarray(np.asarray(L0_ffn_w1_s, np.float32)),
        "s2v": np.ascontiguousarray(np.asarray(L0_ffn_w2_s, np.float32)),
        "sh1v": np.ascontiguousarray(np.asarray(L0_ffn_h1_s, np.float32)),
        "sh2v": np.ascontiguousarray(np.asarray(L0_ffn_h2_s, np.float32)),
        "s3v": np.ascontiguousarray(np.asarray(L0_ffn_w3_s, np.float32)),
    }
    nc = build_nc(T=2048)
    in_maps = [dict(shared, x=np.ascontiguousarray(x[i])) for i in range(n)]
    res = run_bass_kernel_spmd(nc, in_maps, core_ids=list(range(n)))
    return np.stack([np.asarray(r["out"]) for r in res.results], axis=0)


# version 37
# speedup vs baseline: 1.1475x; 1.1475x over previous
import numpy as np
from contextlib import ExitStack

import concourse.bass as bass
import concourse.mybir as mybir
from concourse.bass_utils import run_bass_kernel_spmd
from concourse.alu_op_type import AluOpType

F32 = mybir.dt.float32
BF16 = mybir.dt.bfloat16
AF = mybir.ActivationFunctionType
AX = mybir.AxisListType.X

D = 1024
DC = D // 128
HQ, HKV, HD = 16, 4, 64
DH = 4096
FC = DH // 128
EPS = 1e-5
NEG = -1e30

PAIRS = [(c, c + 4) for c in range(4)] + [(c + 4, c + 8) for c in range(4, 8)]


class Sem:
    def __init__(self, h):
        self.h = h
        self.n = 0

    def inc(self, k=1):
        self.n += k
        return self.n


class Builder:
    ENGS = ("sync", "tensor", "vector", "scalar", "gpsimd")

    def __init__(self, nc):
        self.nc = nc
        self.q = {e: [] for e in self.ENGS}
        self.stack = ExitStack()

    def sem(self, name):
        return Sem(self.stack.enter_context(self.nc.semaphore(name)))

    def emit(self, eng, fn):
        self.q[eng].append(fn)

    def op(self, eng, build, inc=None, inc_n=1):
        if inc is not None:
            inc.inc(inc_n)
            self.emit(eng, lambda e, b=build, s=inc, k=inc_n: b(e).then_inc(s.h, k))
        else:
            self.emit(eng, build)

    def wait(self, eng, sem, val=None):
        v = sem.n if val is None else val
        if v > 0:
            self.emit(eng, lambda e, s=sem, vv=v: e.wait_ge(s.h, vv))

    def dma(self, eng, out_ap, in_ap, sem, **kw):
        sem.inc(16)
        self.emit(
            eng,
            lambda e, o=out_ap, i=in_ap, s=sem, kw=kw: e.dma_start(
                out=o, in_=i, **kw
            ).then_inc(s.h, 16),
        )

    def sbuf(self, name, shape, dtype):
        return self.stack.enter_context(self.nc.sbuf_tensor(name, shape, dtype))

    def psum(self, name, shape, dtype):
        return self.stack.enter_context(self.nc.psum_tensor(name, shape, dtype))


def build_nc(T=2048, ffn_stop=None):
    assert T % 256 == 0
    TB = T // 128
    Tf = T // 2
    TFB = Tf // 128
    NB = T // 512
    SEGf = [(i * 512, min(512, Tf - i * 512)) for i in range((Tf + 511) // 512)]
    nc = bass.Bass("TRN2", target_bir_lowering=False)
    b = Builder(nc)

    x_d = nc.dram_tensor("x", [T, D], F32, kind="ExternalInput").ap()
    wq_d = nc.dram_tensor("wq", [D, HQ * HD], F32, kind="ExternalInput").ap()
    wk_d = nc.dram_tensor("wk", [D, HKV * HD], F32, kind="ExternalInput").ap()
    wv_d = nc.dram_tensor("wv", [D, HKV * HD], F32, kind="ExternalInput").ap()
    wo_d = nc.dram_tensor("wo", [HQ * HD, D], F32, kind="ExternalInput").ap()
    w1_d = nc.dram_tensor("w1", [DH, D], F32, kind="ExternalInput").ap()
    w2_d = nc.dram_tensor("w2", [DH, D], F32, kind="ExternalInput").ap()
    h1_d = nc.dram_tensor("h1", [DH, DH], F32, kind="ExternalInput").ap()
    h2_d = nc.dram_tensor("h2", [DH, DH], F32, kind="ExternalInput").ap()
    w3_d = nc.dram_tensor("w3", [D, DH], F32, kind="ExternalInput").ap()
    s1_d = nc.dram_tensor("s1v", [DH], F32, kind="ExternalInput").ap()
    s2_d = nc.dram_tensor("s2v", [DH], F32, kind="ExternalInput").ap()
    sh1_d = nc.dram_tensor("sh1v", [DH], F32, kind="ExternalInput").ap()
    sh2_d = nc.dram_tensor("sh2v", [DH], F32, kind="ExternalInput").ap()
    s3_d = nc.dram_tensor("s3v", [D], F32, kind="ExternalInput").ap()
    out_d = nc.dram_tensor("out", [T, D], F32, kind="ExternalOutput").ap()
    w1b_d = nc.dram_tensor("w1b", [DH, D], BF16, kind="Internal").ap()
    w2b_d = nc.dram_tensor("w2b", [DH, D], BF16, kind="Internal").ap()
    h1b_d = nc.dram_tensor("h1b", [DH, DH], BF16, kind="Internal").ap()
    h2b_d = nc.dram_tensor("h2b", [DH, DH], BF16, kind="Internal").ap()
    w3b_d = nc.dram_tensor("w3b", [D, DH], BF16, kind="Internal").ap()
    h1t_d = nc.dram_tensor("h1t", [DH, DH], BF16, kind="Internal").ap()
    h2t_d = nc.dram_tensor("h2t", [DH, DH], BF16, kind="Internal").ap()
    w1t_d = nc.dram_tensor("w1t", [D, DH], BF16, kind="Internal").ap()
    w2t_d = nc.dram_tensor("w2t", [D, DH], BF16, kind="Internal").ap()
    w3t_d = nc.dram_tensor("w3t", [DH, D], BF16, kind="Internal").ap()
    lnst_d = nc.dram_tensor("lnst", [T, D], BF16, kind="Internal").ap()
    ln2t_d = nc.dram_tensor("ln2t", [D, T], BF16, kind="Internal").ap()
    x1_d = nc.dram_tensor("x1st", [T, D], F32, kind="Internal").ap()

    PO = {}
    off = 0
    for name, sz in [
        ("q", 8 * T), ("ctx", 8 * T),
        ("lnfm", 8 * T), ("k", 2 * T), ("v", TB * 256),
        ("pbb", 4 * 1024), ("pt", 2 * 1024), ("lnt", 2 * 1024),
        ("wq", 8 * 1024), ("wk", 2 * 1024), ("wv", 2 * 1024), ("wo", 8 * 1024),
    ]:
        PO[name] = off
        off += sz
    GO = 0
    HO = 16 * T
    TPO = HO + FC * Tf
    if DC * Tf + 16384 <= FC * Tf:
        TP12O = HO + DC * Tf
    else:
        TP12O = TPO + 16384
    POOL_N = max(off, TPO + 16384, HO + FC * 1024, TP12O + 16384)
    pool = b.sbuf("pool", [128, POOL_N], BF16)

    def P(name, lo, n):
        return pool[:, PO[name] + lo: PO[name] + lo + n]

    f32pool = b.sbuf("f32pool", [128, 8192], F32)
    xbuf = [f32pool[:, i * 1024:(i + 1) * 1024] for i in range(2)]
    pbf = [f32pool[:, 2048 + i * 1024: 2048 + (i + 1) * 1024] for i in range(4)]
    x1t = [f32pool[:, 6144 + i * 1024: 6144 + (i + 1) * 1024] for i in range(2)]
    x1t.append(f32pool[:, 5120:6144])
    silbuf = f32pool[:, 0:1024]
    w3tmp = f32pool[:, 1024:2048]
    outbuf = [f32pool[:, 2048 + i * 1024: 2048 + (i + 1) * 1024] for i in range(2)]
    w3s_bc = f32pool[:, 4096:5120]

    w3s_row = b.sbuf("w3s_row", [1, 1024], F32)
    sstage = f32pool[0:32, 6144:6656]
    maskT = b.sbuf("maskT", [128, 128], F32)[:, :]
    identb = b.sbuf("identb", [128, 128], BF16)[:, :]
    identf = b.sbuf("identf", [128, 128], F32)[:, :]
    onesf = b.sbuf("onesf", [1, 128], F32)[:, :]
    onesplus = b.sbuf("onesplus", [128, 192], BF16)[:, :]
    s1s = b.sbuf("s1s", [128, FC], F32)
    s2s = b.sbuf("s2s", [128, FC], F32)
    sh1s = b.sbuf("sh1s", [128, FC], F32)
    sh2s = b.sbuf("sh2s", [128, FC], F32)
    stats = b.sbuf("stats", [128, 20], F32)
    asm = b.sbuf("asm", [128, 16], F32)

    ps = [b.psum("ps_a", [128, 1024], F32),
          b.psum("ps_b", [128, 1024], F32),
          b.psum("ps_c", [128, 1024], F32)]
    ps_ctx = b.psum("ps_ctx", [128, 512], F32)
    ps_bc = b.psum("ps_bc", [128, 512], F32)

    sPE = b.sem("sPE")
    sACT = b.sem("sACT")
    sDVE = b.sem("sDVE")
    sGPS = b.sem("sGPS")
    dX = [b.sem(f"dX{i}") for i in range(4)]
    dXr = [b.sem(f"dXr{i}") for i in range(3)]
    dLN = [b.sem(f"dLN{i}") for i in range(2)]
    dLN2 = [b.sem(f"dLN2{i}") for i in range(2)]
    dLNi = b.sem("dLNi")
    dLNc = [b.sem(f"dLNc{i}") for i in range(DC)]
    dW3g = [b.sem(f"dW3g{i}") for i in range(8)]
    dLT = [b.sem(f"dLT{i}") for i in range(2)]
    dW = b.sem("dW")
    dCast = {w: b.sem(f"dC_{w}") for w in ("w1", "w2", "h1", "h2", "w3")}
    dSW = [b.sem(f"dSW{i}") for i in range(2)]
    dSH = [b.sem(f"dSH{i}") for i in range(2)]
    dMX = [b.sem(f"dMX{i}") for i in range(3)]
    dMS = [b.sem(f"dMS{i}") for i in range(3)]
    dW3 = b.sem("dW3")
    dX1 = [b.sem(f"dX1{i}") for i in range(3)]
    dX1r = [b.sem(f"dX1r{i}") for i in range(2)]
    dO = b.sem("dO")

    xb4 = [xbuf[0], xbuf[1], pbf[1], pbf[2]]
    for tb in range(min(4, TB)):
        b.dma("sync", xb4[tb], x_d[tb * 128:(tb + 1) * 128, :], dX[tb % 4])

    b.op("gpsimd", lambda e: e.memset(maskT, 0.0), inc=sGPS)
    b.wait("gpsimd", sGPS)
    b.op("gpsimd", lambda e: e.affine_select(
        maskT, maskT, compare_op=AluOpType.is_ge, fill=NEG, base=0,
        pattern=[[1, 128]], channel_multiplier=-1), inc=sGPS)
    b.op("gpsimd", lambda e: e.memset(onesplus, 1.0), inc=sGPS)
    b.wait("gpsimd", sGPS)
    b.op("gpsimd", lambda e: e.memset(onesplus[:, 64:128], 0.0), inc=sGPS)
    b.op("gpsimd", lambda e: e.memset(
        pool[:, PO["v"]: PO["v"] + TB * 448], 0.0), inc=sGPS)
    b.op("gpsimd", lambda e: e.memset(identb, 0.0), inc=sGPS)
    b.wait("gpsimd", sGPS)
    b.op("gpsimd", lambda e: e.affine_select(
        identb, identb, compare_op=AluOpType.not_equal, fill=1.0, base=0,
        pattern=[[-1, 128]], channel_multiplier=1), inc=sGPS)
    b.op("gpsimd", lambda e: e.memset(identf, 0.0), inc=sGPS)
    b.wait("gpsimd", sGPS)
    b.op("gpsimd", lambda e: e.affine_select(
        identf, identf, compare_op=AluOpType.not_equal, fill=1.0, base=0,
        pattern=[[-1, 128]], channel_multiplier=1), inc=sGPS)
    b.op("gpsimd", lambda e: e.memset(onesf, 1.0), inc=sGPS)
    b.op("gpsimd", lambda e: e.memset(stats[:, 16:17], EPS), inc=sGPS)
    GPS_CONST = sGPS.n

    for ic in range(DC):
        b.dma("gpsimd", P("wq", ic * 1024, 1024), wq_d[ic * 128:(ic + 1) * 128, :], dW)
        b.dma("gpsimd", P("wk", ic * 256, 256), wk_d[ic * 128:(ic + 1) * 128, :], dW)
        b.dma("gpsimd", P("wv", ic * 256, 256), wv_d[ic * 128:(ic + 1) * 128, :], dW)
    for c in range(8):
        ha, hb = PAIRS[c]
        b.dma("gpsimd", P("wo", c * 1024, 1024)[0:64, :],
              wo_d[ha * 64:(ha + 1) * 64, :], dW)
        b.dma("gpsimd", P("wo", c * 1024, 1024)[64:128, :],
              wo_d[hb * 64:(hb + 1) * 64, :], dW)
    for i_, s_dd in enumerate((s1_d, s2_d, sh1_d, sh2_d)):
        b.dma("gpsimd", sstage[:, i_ * 128:(i_ + 1) * 128],
              s_dd.rearrange("(r c) -> r c", r=32), dW)
    b.dma("gpsimd", w3s_row[:, :], s3_d.rearrange("(a o) -> a o", a=1), dW)
    DW_SETUP = dW.n
    b.wait("tensor", dW, DW_SETUP)
    b.wait("tensor", sGPS, GPS_CONST)
    for i_, s_sb in enumerate((s1s, s2s, sh1s, sh2s)):
        b.op("tensor", lambda e, i_=i_: e.transpose(
            ps_ctx[:, i_ * 32:(i_ + 1) * 32],
            sstage[:, i_ * 128:(i_ + 1) * 128], identf[0:32, 0:32]), inc=sPE)
    b.wait("vector", sPE)
    for i_, s_sb in enumerate((s1s, s2s, sh1s, sh2s)):
        b.op("vector", lambda e, i_=i_, o=s_sb: e.tensor_copy(
            o[:, 0:FC], ps_ctx[:, i_ * 32: i_ * 32 + FC]), inc=sDVE)
    SCALES_RDY = sDVE.n

    def cast_w(src, dst, rows, piece, sem):
        for r0 in range(0, rows, piece):
            b.dma("gpsimd", dst[r0:r0 + piece, :], src[r0:r0 + piece, :], sem)

    def emit_ln_head(x_ap, sq_scratch, wait_dve, wait_act, sb):
        def S(c):
            return stats[:, sb + c: sb + c + 1]
        for sem, val in wait_dve:
            b.wait("vector", sem, val)
        for sem, val in wait_act:
            b.wait("scalar", sem, val)
        b.op("vector", lambda e: e.reduce_sum(S(0), x_ap, axis=AX), inc=sDVE)
        b.op("scalar", lambda e: e.activation(
            sq_scratch, x_ap, AF.Square, accum_out=S(1)), inc=sACT)
        sq_done = sACT.n
        b.wait("scalar", sDVE)
        b.op("scalar", lambda e: e.activation(
            S(2), S(0), AF.Copy, scale=1.0 / D), inc=sACT)
        b.wait("vector", sACT)
        b.op("vector", lambda e: e.tensor_scalar(
            S(3), S(2), -1.0, None, AluOpType.mult), inc=sDVE)
        b.wait("vector", sDVE)
        b.op("vector", lambda e: e.scalar_tensor_tensor(
            S(4), S(3), S(2), stats[:, 16:17],
            AluOpType.mult, AluOpType.add), inc=sDVE)
        b.wait("scalar", sDVE)
        b.wait("scalar", sACT, sq_done)
        b.op("scalar", lambda e: e.activation(
            S(5), S(1), AF.Ln, scale=1.0 / D, bias=S(4)), inc=sACT)
        b.wait("scalar", sACT)
        b.op("scalar", lambda e: e.activation(
            S(6), S(5), AF.Exp, scale=-0.5), inc=sACT)
        return sACT.n, sDVE.n

    def emit_ln_tail(x_ap, out_ap, sb, exp_cnt, dve_cnt, wait_out):
        def S(c):
            return stats[:, sb + c: sb + c + 1]
        b.wait("vector", sACT, exp_cnt)
        b.wait("vector", sDVE, dve_cnt)
        for sem, val in wait_out:
            b.wait("vector", sem, val)
        b.op("vector", lambda e: e.tensor_tensor(
            S(7), S(3), S(6), AluOpType.mult), inc=sDVE)
        b.wait("vector", sDVE)
        b.op("vector", lambda e: e.tensor_scalar(
            out_ap, x_ap, S(6), S(7), AluOpType.mult, AluOpType.add),
            inc=sDVE)
        return (sDVE, sDVE.n)

    lnt = [P("lnt", i * 1024, 1024) for i in range(2)]
    b.wait("scalar", sGPS, GPS_CONST)
    b.wait("vector", sGPS, GPS_CONST)
    lnp = {0: (0, 0), 1: (0, 0)}

    def p1_tail(pd_):
        tb_, j_, s_ = pd_["tb"], pd_["j"], pd_["s"]
        nrm = emit_ln_tail(xb4[j_], lnt[s_], 8 * s_, pd_["exp"], pd_["dve"],
                           [(dLN[s_], pd_["lnwar"])])
        b.wait("sync", nrm[0], nrm[1])
        b.dma("sync", lnst_d[tb_ * 128:(tb_ + 1) * 128, :], lnt[s_], dLN[s_])
        if tb_ + 4 < TB:
            b.dma("sync", xb4[j_],
                  x_d[(tb_ + 4) * 128:(tb_ + 5) * 128, :], dX[j_])

    pend1 = None
    for tb in range(TB):
        j = tb % 4
        s = tb % 2
        pa, pd = lnp[s]
        wd = [(dX[j], 16 * (tb // 4 + 1)), (sACT, pa)]
        wa = [(dX[j], 16 * (tb // 4 + 1)), (sDVE, pd)]
        lnwar = dLN[s].n
        exp_cnt, dve_cnt = emit_ln_head(xb4[j], pbf[0], wd, wa, 8 * s)
        lnp[s] = (exp_cnt, dve_cnt)
        cur = {"tb": tb, "j": j, "s": s, "exp": exp_cnt, "dve": dve_cnt,
               "lnwar": lnwar}
        if pend1 is not None:
            p1_tail(pend1)
        pend1 = cur
    p1_tail(pend1)
    p1_dve = sDVE.n
    for sem in dLN:
        b.wait("sync", sem)
    ln1_rdy = []
    for ic in range(DC):
        b.dma("sync", P("lnfm", ic * T, T),
              lnst_d[:, ic * 128:(ic + 1) * 128], dLNc[ic], transpose=True)
        ln1_rdy.append(dLNc[ic].n)
    for ic in range(DC):
        b.wait("gpsimd", dLNc[ic], ln1_rdy[ic])
    cast_w(w1_d, w1b_d, DH, 1024, dCast["w1"])
    cast_w(w2_d, w2b_d, DH, 1024, dCast["w2"])
    cast_w(h1_d, h1b_d, DH, 512, dCast["h1"])
    cast_w(h2_d, h2b_d, DH, 512, dCast["h2"])
    cast_w(w3_d, w3b_d, D, 512, dCast["w3"])

    psum_half = []
    for t_ in ps:
        psum_half.append(t_[:, 0:512])
        psum_half.append(t_[:, 512:1024])
    slot_free = [0] * 6
    si = 0

    b.wait("tensor", dW, DW_SETUP)

    for c in range(8):
        ha, hb = PAIRS[c]
        for nb in range(NB):
            slot = psum_half[si]
            b.wait("tensor", sACT, slot_free[si])
            for idx, h in ((0, ha), (1, hb)):
                for i in range(DC):
                    if c == 0 and nb == 0 and idx == 0:
                        b.wait("tensor", dLNc[i], ln1_rdy[i])
                    b.op("tensor", lambda e, l=P("wq", i * 1024 + h * 64, 64),
                         r=P("lnfm", i * T + nb * 512, 512),
                         o=slot[idx * 64:(idx + 1) * 64, 0:512],
                         st=(i == 0), sp=(i == DC - 1), tp=(0, idx * 64):
                         e.matmul(o, l, r, start=st, stop=sp, tile_position=tp),
                         inc=sPE)
            b.wait("scalar", sPE)
            b.op("scalar", lambda e, o=P("q", c * T + nb * 512, 512), s=slot:
                 e.activation(o, s, AF.Copy, scale=0.125), inc=sACT)
            slot_free[si] = sACT.n
            si = (si + 1) % 6

    def proj(lhsT_list, rhs_list, out_to, n_cols, out_multi=None):
        nonlocal si
        slot = psum_half[si]
        b.wait("tensor", sACT, slot_free[si])
        for i, (l, r) in enumerate(zip(lhsT_list, rhs_list)):
            b.op("tensor", lambda e, l=l, r=r, o=slot[:, 0:n_cols],
                 st=(i == 0), sp=(i == len(lhsT_list) - 1):
                 e.matmul(o, l, r, start=st, stop=sp), inc=sPE)
        b.wait("scalar", sPE)
        if out_multi is not None:
            for dst, plo, w in out_multi:
                b.op("scalar", lambda e, o=dst, s=slot[:, plo:plo + w]:
                     e.activation(o, s, AF.Copy), inc=sACT)
        else:
            b.op("scalar", lambda e, o=out_to, s=slot[:, 0:n_cols]:
                 e.activation(o, s, AF.Copy), inc=sACT)
        slot_free[si] = sACT.n
        si = (si + 1) % 6

    for kc in range(2):
        for nb in range(NB):
            proj([P("wk", ic * 256 + kc * 128, 128) for ic in range(DC)],
                 [P("lnfm", ic * T + nb * 512, 512) for ic in range(DC)],
                 P("k", kc * T + nb * 512, 512), 512)
    VDATA = {0: 0, 1: 256, 2: 128, 3: 384}
    for tb in range(TB):
        proj([P("lnfm", ic * T + tb * 128, 128) for ic in range(DC)],
             [P("wv", ic * 256, 256) for ic in range(DC)],
             None, 256,
             out_multi=[
                 (pool[:, PO["v"] + tb * 448 + VDATA[kv]:
                       PO["v"] + tb * 448 + VDATA[kv] + 64], kv * 64, 64)
                 for kv in range(4)])
    qkv_pe = sPE.n

    b.wait("gpsimd", sDVE, p1_dve)
    for sem in dX:
        b.wait("gpsimd", sem)
    b.dma("gpsimd", xbuf[0], x_d[0:128, :], dXr[0])
    b.dma("gpsimd", xbuf[1], x_d[128:256, :], dXr[1])

    mstage = [P("wq", i * 4096, 4096) for i in range(2)]
    b.wait("sync", sPE)
    MS_done = {}
    u = 0
    for wname, wb, wt, nic, rr in (
            ("w1", w1b_d, w1t_d, DC, DH), ("w2", w2b_d, w2t_d, DC, DH),
            ("h1", h1b_d, h1t_d, FC, DH), ("h2", h2b_d, h2t_d, FC, DH),
            ("w3", w3b_d, w3t_d, FC, D)):
        b.wait("sync", dCast[wname])
        for ic in range(nic):
            par = u % 2
            if u >= 2:
                b.wait("sync", dMS[par], dMS[par].n)
            b.dma("sync", mstage[par][:, 0:rr], wb[:, ic * 128:(ic + 1) * 128],
                  dMX[par], transpose=True)
            b.wait("sync", dMX[par], dMX[par].n)
            b.dma("sync", wt[ic * 128:(ic + 1) * 128, :], mstage[par][:, 0:rr],
                  dMS[par])
            u += 1
        MS_done[wname] = (dMS[0].n, dMS[1].n, dMS[2].n)
    MS_ALL = (dMS[0].n, dMS[1].n, dMS[2].n)

    QGN = T // 512
    pt_strip = [P("pt", i * 512, 512) for i in range(4)]
    sA = [ps[0][:, 0:512], ps[0][:, 512:1024]]
    sB = [ps[1][:, 0:512], ps[1][:, 512:1024]]
    ctxb = [ps[2][:, 0:512], ps[2][:, 512:1024]]
    rsb = [ps_ctx, ps_bc]
    bcast_sb = w3s_bc[:, 0:512]

    PVOFF = {0: 0, 2: 128, 1: 192, 3: 320}

    def Pv(tb_, off_, w_):
        return pool[:, PO["v"] + tb_ * 448 + off_: PO["v"] + tb_ * 448 + off_ + w_]

    b.wait("tensor", sDVE, SCALES_RDY)
    b.wait("tensor", sACT)
    s_free = [[0, 0], [0, 0]]
    pt_free = [0, 0, 0, 0]
    ctx_free = [0, 0]
    rs_free = [0, 0]
    it = 0
    ctx_evac = sDVE.n

    for qbg in range(QGN):
        for c in range(8):
            ha, hb = PAIRS[c]
            units = [(0, ha), (1, hb)]
            kmax = qbg * 4 + 4
            sdone = {}
            mdone = {}
            edone = {}
            for kbp in range(kmax // 2):
                kbs = (2 * kbp, 2 * kbp + 1)
                for kb in kbs:
                    par = kb % 2
                    for u, (idx, h) in enumerate(units):
                        sbank = (sA if u == 0 else sB)[par]
                        base = 64 * idx
                        kv = h // 4
                        kchunk, kbase = kv // 2, 64 * (kv % 2)
                        b.wait("tensor", sACT, s_free[u][par])
                        b.op("tensor", lambda e, o=sbank,
                             l=P("k", kchunk * T + kb * 128, 128)[kbase:kbase + 64, :],
                             rr=P("q", c * T + qbg * 512, 512)[base:base + 64, :],
                             tp=(base, 0):
                             e.matmul(o, l, rr, start=True, stop=True,
                                      tile_position=tp), inc=sPE)
                        sdone[(u, kb)] = sPE.n
                for kb in kbs:
                    par = kb % 2
                    r = kb - qbg * 4
                    w0 = r * 128 if r >= 0 else 0
                    if r >= 0:
                        for u in range(2):
                            sbank = (sA if u == 0 else sB)[par]
                            b.wait("vector", sPE, sdone[(u, kb)])
                            b.op("vector", lambda e,
                                 s=sbank[:, w0:w0 + 128]:
                                 e.tensor_tensor(s, s, maskT, AluOpType.add),
                                 inc=sDVE)
                            mdone[(u, kb)] = sDVE.n
                    for u in range(2):
                        sbank = (sA if u == 0 else sB)[par]
                        strip = pt_strip[u * 2 + par]
                        b.wait("scalar", sPE, sdone[(u, kb)])
                        if r >= 0:
                            b.wait("scalar", sDVE, mdone[(u, kb)])
                        b.wait("scalar", sPE, pt_free[u * 2 + par])
                        if w0 > 0:
                            b.op("scalar", lambda e, o=strip[:, 0:w0],
                                 s=sbank[:, 0:w0]:
                                 e.activation(o, s, AF.Copy, scale=0.0),
                                 inc=sACT)
                        b.op("scalar", lambda e, o=strip[:, w0:512],
                             s=sbank[:, w0:512]:
                             e.activation(o, s, AF.Exp), inc=sACT)
                        s_free[u][par] = sACT.n
                        edone[(u, kb)] = sACT.n
                for kb in kbs:
                    par = kb % 2
                    for u, (idx, h) in enumerate(units):
                        kv = h // 4
                        strip = pt_strip[u * 2 + par]
                        b.wait("tensor", sACT, edone[(u, kb)])
                        if kb == 0 and u == 0:
                            b.wait("tensor", sDVE, ctx_free[it % 2])
                            b.wait("tensor", sDVE, rs_free[it % 2])
                        st_ = (kb == 0 and u == 0)
                        sp_ = (kb == kmax - 1 and u == 1)
                        b.op("tensor", lambda e, o=ctxb[it % 2],
                             l=Pv(kb, PVOFF[kv], 128), rr=strip,
                             st_=st_, sp_=sp_:
                             e.matmul(o, l, rr, start=st_, stop=sp_), inc=sPE)
                        b.op("tensor", lambda e, o=rsb[it % 2][:, :],
                             l=onesplus[:, 64 * u: 64 * u + 128], rr=strip,
                             st_=st_, sp_=sp_:
                             e.matmul(o, l, rr, start=st_, stop=sp_), inc=sPE)
                        pt_free[u * 2 + par] = sPE.n
            pv_done = sPE.n
            b.wait("vector", sPE, pv_done)
            b.op("vector", lambda e, i=rsb[it % 2][:, :]:
                 e.reciprocal(bcast_sb, i), inc=sDVE)
            rs_free[it % 2] = sDVE.n
            b.op("vector", lambda e, o=P("ctx", c * T + qbg * 512, 512),
                 i=ctxb[it % 2]: e.tensor_tensor(
                     o, i, bcast_sb, AluOpType.mult), inc=sDVE)
            ctx_free[it % 2] = sDVE.n
            ctx_evac = sDVE.n
            it += 1
    attn_pe = sPE.n

    b.wait("tensor", dW, DW_SETUP)
    b.wait("tensor", sACT)
    b.wait("tensor", sDVE)
    for nh in range(2):
        b.op("tensor", lambda e, nh=nh: e.matmul(
            ps[2][:, nh * 512:(nh + 1) * 512], onesf,
            w3s_row[:, nh * 512:(nh + 1) * 512], start=True, stop=True),
            inc=sPE)
    bcast_mm = sPE.n
    pre_dve = sDVE.n
    b.wait("vector", sPE, bcast_mm)
    b.wait("vector", sACT)
    b.wait("vector", sGPS)
    b.op("vector", lambda e: e.tensor_copy(w3s_bc, ps[2][:, 0:1024]), inc=sDVE)
    W3SBC = sDVE.n

    tp12 = [pool[:, TP12O + i * 8192: TP12O + (i + 1) * 8192] for i in range(2)]
    gran12_free = [(None, 0), (None, 0)]
    NG12 = DH // 512
    w1t3 = w1t_d.rearrange("(ib p) o -> p ib o", p=128)
    w2t3 = w2t_d.rearrange("(ib p) o -> p ib o", p=128)
    w3t3 = w3t_d.rearrange("(ib p) o -> p ib o", p=128)
    ln2t3 = ln2t_d.rearrange("(ib p) o -> p ib o", p=128)
    w3_last = {}

    W12_HOIST = TP12O == HO + DC * Tf and 4 * T + 8192 <= 8 * T

    def w12_loads(g, w12_first, g12cnt):
        par = g % 2
        gate = MS_done["w2"] if W12_HOIST else MS_ALL
        for q_ in range(3):
            b.wait("gpsimd", dMS[q_], gate[q_])
        eng, cnt = gran12_free[par]
        if eng is not None:
            b.wait("gpsimd", eng, cnt)
        if g in w12_first:
            b.wait("gpsimd", sPE, w12_first[g])
        b.dma("gpsimd",
              tp12[par][:, 0:4096].rearrange("p (i o) -> p i o", o=512),
              w1t3[:, :, g * 512:(g + 1) * 512], dSW[par])
        b.dma("gpsimd",
              tp12[par][:, 4096:8192].rearrange("p (i o) -> p i o", o=512),
              w2t3[:, :, g * 512:(g + 1) * 512], dSW[par])
        g12cnt[g] = dSW[par].n

    g12cnt_p1 = {}
    if W12_HOIST:
        w12_loads(0, {0: qkv_pe, 1: attn_pe}, g12cnt_p1)
        w12_loads(1, {0: qkv_pe, 1: attn_pe}, g12cnt_p1)

    wo_slot_free = [pre_dve, pre_dve, sDVE.n]
    x1_free = [(sACT, sACT.n, 0)] * 3
    lnp4 = {0: (sACT.n, sDVE.n), 1: (sACT.n, sDVE.n)}
    P3_GPS = sGPS.n
    pend4 = None
    TR_P1 = W12_HOIST and Tf == 1024
    psTb = [ps_ctx.bitcast(BF16), ps_bc.bitcast(BF16)]
    psT_war = [0, 0]
    lnt_tr_war = {0: 0, 1: 0}
    ln2cnt = {}

    def p4_tail(pd_):
        tb_, jx_, j2_ = pd_["tb"], pd_["jx"], pd_["j2"]
        nrm = emit_ln_tail(x1t[jx_], lnt[j2_], 8 * j2_, pd_["exp"],
                           pd_["dve"], [(dLN2[j2_], pd_["lnwar"]),
                                        (sPE, lnt_tr_war[j2_])])
        x1_free[jx_] = (nrm[0], nrm[1], pd_["dx1"])
        b.wait("gpsimd", nrm[0], nrm[1])
        b.dma("gpsimd", lnst_d[tb_ * 128:(tb_ + 1) * 128, :], lnt[j2_],
              dLN2[j2_])
        if TR_P1 and tb_ < TFB:
            pT = psTb[tb_ % 2]
            b.wait("tensor", nrm[0], nrm[1])
            b.wait("tensor", sACT, psT_war[tb_ % 2])
            for ic in range(DC):
                b.op("tensor", lambda e, o=pT[:, ic * 128:(ic + 1) * 128],
                     i=lnt[j2_][:, ic * 128:(ic + 1) * 128]:
                     e.transpose(o, i, identb), inc=sPE)
            lnt_tr_war[j2_] = sPE.n
            b.wait("scalar", sPE, lnt_tr_war[j2_])
            for ic in range(DC):
                b.op("scalar", lambda e,
                     o=pool[:, HO + ic * Tf + tb_ * 128:
                            HO + ic * Tf + tb_ * 128 + 128],
                     i=pT[:, ic * 128:(ic + 1) * 128]:
                     e.copy(o, i), inc=sACT)
                ln2cnt[(tb_, ic)] = sACT.n
            psT_war[tb_ % 2] = sACT.n

    b.wait("vector", dW, DW_SETUP)
    for tb in range(TB):
        j3, j2, jx = tb % 2, tb % 2, tb % 3
        slot = ps[tb % 3]
        b.wait("tensor", sDVE, ctx_evac)
        b.wait("tensor", sDVE, wo_slot_free[tb % 3])
        for nh in range(2):
            for i in range(DC):
                b.op("tensor", lambda e, o=slot[:, nh * 512:(nh + 1) * 512],
                     l=P("ctx", i * T + tb * 128, 128),
                     r=P("wo", i * 1024 + nh * 512, 512),
                     st=(i == 0), sp=(i == DC - 1):
                     e.matmul(o, l, r, start=st, stop=sp), inc=sPE)
        wo_done = sPE.n
        b.wait("vector", sPE, wo_done)
        b.wait("vector", dXr[j3], 16 * (tb // 2 + 1))
        lnsem, lnr, dmr = x1_free[jx]
        b.wait("vector", lnsem, lnr)
        b.wait("vector", dX1[jx], dmr)
        if tb == 2:
            b.wait("vector", sGPS, P3_GPS)
        b.op("vector", lambda e, o=x1t[jx], p=slot[:, 0:1024], x=xbuf[j3]:
             e.tensor_tensor(o, p, x, AluOpType.add), inc=sDVE)
        wo_slot_free[tb % 3] = sDVE.n
        x1_done = sDVE.n
        b.wait("gpsimd", sDVE, x1_done)
        b.dma("gpsimd", x1_d[tb * 128:(tb + 1) * 128, :], x1t[jx], dX1[jx])
        pa, pd = lnp4[j2]
        wd = [(sDVE, x1_done), (sACT, pa)]
        wa = [(sDVE, x1_done), (sDVE, pd), (sGPS, P3_GPS)]
        lnwar = dLN2[j2].n
        exp_cnt, dve_cnt = emit_ln_head(x1t[jx], pbf[0], wd, wa, 8 * j2)
        lnp4[j2] = (exp_cnt, dve_cnt)
        cur = {"tb": tb, "jx": jx, "j2": j2, "exp": exp_cnt, "dve": dve_cnt,
               "lnwar": lnwar, "dx1": dX1[jx].n}
        if pend4 is not None:
            p4_tail(pend4)
        pend4 = cur
        if tb + 2 < TB:
            b.wait("gpsimd", sDVE, x1_done)
            b.dma("gpsimd", xbuf[j3], x_d[(tb + 2) * 128:(tb + 3) * 128, :], dXr[j3])
    p4_tail(pend4)

    p4_pe = sPE.n
    tp_buf = [pool[:, PO["wk"] + i * 4096: PO["wk"] + (i + 1) * 4096]
              for i in range(2)]
    w3T = pool[:, HO: HO + FC * 1024]
    LN2HO = HO

    gran_free = [(None, 0), (None, 0)]
    pf = {0: (sDVE, sDVE.n), 1: (sDVE, sDVE.n), 2: (sDVE, sDVE.n)}

    for p_ in range(2):
        toff = p_ * Tf
        ln2h_rdy = []
        if p_ == 0 and TR_P1:
            for ic in range(DC):
                ln2h_rdy.append([(sACT, ln2cnt[(3, ic)]),
                                 (sACT, ln2cnt[(7, ic)])])
        elif p_ == 1 and W12_HOIST:
            for q_ in range(2):
                b.wait("gpsimd", dMS[q_], LT_MS[q_])
            for hf in range(2):
                b.wait("gpsimd", sPE, w3_last[4 * hf + 3])
                b.dma("gpsimd",
                      pool[:, LN2HO + hf * 4 * Tf: LN2HO + (hf + 1) * 4 * Tf]
                      .rearrange("p (i o) -> p i o", o=Tf),
                      ln2t3[:, hf * 4:(hf + 1) * 4, toff:toff + Tf], dLT[hf])
                for _ in range(4):
                    ln2h_rdy.append([(dLT[hf], dLT[hf].n)])
        else:
            if p_ == 0 and W12_HOIST:
                ntile = TB // 2
                b.wait("sync", dLN2[0], 16 * ((ntile + 1) // 2))
                b.wait("sync", dLN2[1], 16 * (ntile // 2))
                b.wait("sync", sPE, qkv_pe)
            else:
                b.wait("sync", dLN2[0])
                b.wait("sync", dLN2[1])
                b.wait("sync", sPE)
            for ic in range(DC):
                b.dma("sync", pool[:, LN2HO + ic * Tf: LN2HO + (ic + 1) * Tf],
                      lnst_d[toff:toff + Tf, ic * 128:(ic + 1) * 128],
                      dLNc[ic], transpose=True)
                ln2h_rdy.append([(dLNc[ic], dLNc[ic].n)])
        if p_ == 0 and W12_HOIST:
            x1b = [x1t[0].bitcast(BF16), x1t[1].bitcast(BF16)]
            for sem in dX1:
                b.wait("sync", sem)
            b.wait("sync", sDVE)
            b.wait("sync", sACT)
            lt_ms = {}
            for ic in range(DC):
                par = ic % 2
                if ic >= 2:
                    b.wait("sync", dMS[par], lt_ms[ic - 2])
                b.dma("sync", x1b[par], lnst_d[:, ic * 128:(ic + 1) * 128],
                      dMX[par], transpose=True)
                b.wait("sync", dMX[par], dMX[par].n)
                b.dma("sync", ln2t_d[ic * 128:(ic + 1) * 128, :], x1b[par],
                      dMS[par])
                lt_ms[ic] = dMS[par].n
            LT_MS = (dMS[0].n, dMS[1].n)
        if p_ == 0 and W12_HOIST:
            g12cnt = g12cnt_p1
        else:
            g12cnt = {}
            if p_ == 1 and W12_HOIST:
                first = {0: w3_last[15], 1: w3_last[23]}
            else:
                cur = sPE.n
                first = {0: cur, 1: cur}
            w12_loads(0, first, g12cnt)
            w12_loads(1, first, g12cnt)
        for g in range(NG12):
            par = g % 2
            for oc4 in range(4):
                oc = g * 4 + oc4
                b.wait("tensor", dSW[par], g12cnt[g])
                b.wait("tensor", pf[0][0], pf[0][1])
                for sseg, (so, sw) in enumerate(SEGf):
                    for i in range(DC):
                        if g == 0 and oc4 == 0 and sseg < len(ln2h_rdy[i]):
                            sem_, val_ = ln2h_rdy[i][sseg]
                            b.wait("tensor", sem_, val_)
                        b.op("tensor", lambda e,
                             o=ps[0][:, so:so + sw],
                             l=tp12[par][:, i * 512 + oc4 * 128: i * 512 + oc4 * 128 + 128],
                             r=pool[:, LN2HO + i * Tf + so: LN2HO + i * Tf + so + sw],
                             st=(i == 0), sp=(i == DC - 1):
                             e.matmul(o, l, r, start=st, stop=sp), inc=sPE)
                a_done = sPE.n
                b.wait("tensor", pf[1][0], pf[1][1])
                for so, sw in SEGf:
                    for i in range(DC):
                        b.op("tensor", lambda e,
                             o=ps[1][:, so:so + sw],
                             l=tp12[par][:, 4096 + i * 512 + oc4 * 128: 4096 + i * 512 + oc4 * 128 + 128],
                             r=pool[:, LN2HO + i * Tf + so: LN2HO + i * Tf + so + sw],
                             st=(i == 0), sp=(i == DC - 1):
                             e.matmul(o, l, r, start=st, stop=sp), inc=sPE)
                b_done = sPE.n
                b.wait("scalar", sPE, a_done)
                b.wait("scalar", sDVE)
                b.op("scalar", lambda e, oc=oc, s=ps[0][:, 0:Tf]:
                     e.activation(silbuf[:, 0:Tf], s, AF.Silu,
                                  scale=s1s[:, oc:oc + 1]), inc=sACT)
                pf[0] = (sACT, sACT.n)
                if oc4 == 0 and g + 1 < NG12 and (g + 1) not in g12cnt:
                    w12_loads(g + 1, {}, g12cnt)
                b.wait("vector", sPE, b_done)
                b.wait("vector", sACT)
                b.op("vector", lambda e, oc=oc,
                     o=pool[:, GO + oc * Tf: GO + (oc + 1) * Tf],
                     bb=ps[1][:, 0:Tf]:
                     e.scalar_tensor_tensor(o, bb, s2s[:, oc:oc + 1],
                                            silbuf[:, 0:Tf],
                                            AluOpType.mult, AluOpType.mult),
                     inc=sDVE)
                pf[1] = (sDVE, sDVE.n)
            gran12_free[par] = (sPE, sPE.n)
        g_ready = (sDVE, sDVE.n)
        if ffn_stop == ("g" if p_ == 0 else "g2"):
            b.wait("sync", g_ready[0], g_ready[1])
            break

        def hidden_layer(src_off, dst_off, wtd, ms_done, s_sb, src_ready):
            NGH = DH // 128
            hcnt = {}
            wt3 = wtd.rearrange("(ib p) o -> p ib o", p=128)

            def h_loads(g):
                par = g % 2
                for q_ in range(3):
                    b.wait("gpsimd", dMS[q_], ms_done[q_])
                eng, cnt = gran_free[par]
                if eng is None:
                    b.wait("gpsimd", dLN2[0])
                    b.wait("gpsimd", dLN2[1])
                    b.wait("gpsimd", sPE, p4_pe)
                else:
                    b.wait("gpsimd", eng, cnt)
                b.dma("gpsimd",
                      tp_buf[par].rearrange("p (i o) -> p i o", o=128),
                      wt3[:, :, g * 128:(g + 1) * 128], dSH[par])
                hcnt[g] = dSH[par].n

            h_loads(0)
            h_loads(1)
            for g in range(NGH):
                par = g % 2
                oc = g
                psl = oc % 2
                pslot = ps[psl]
                b.wait("tensor", dSH[par], hcnt[g])
                b.wait("tensor", pf[psl][0], pf[psl][1])
                b.wait("tensor", src_ready[0], src_ready[1])
                for so, sw in SEGf:
                    for i in range(FC):
                        b.op("tensor", lambda e,
                             o=pslot[:, so:so + sw],
                             l=tp_buf[par][:, i * 128: i * 128 + 128],
                             r=pool[:, src_off + i * Tf + so: src_off + i * Tf + so + sw],
                             st=(i == 0), sp=(i == FC - 1):
                             e.matmul(o, l, r, start=st, stop=sp), inc=sPE)
                mm_done = sPE.n
                b.wait("scalar", sPE, mm_done)
                b.op("scalar", lambda e, oc=oc,
                     o=pool[:, dst_off + oc * Tf: dst_off + (oc + 1) * Tf],
                     s=pslot[:, 0:Tf]:
                     e.activation(o, s, AF.Gelu, scale=s_sb[:, oc:oc + 1]),
                     inc=sACT)
                pf[psl] = (sACT, sACT.n)
                gran_free[par] = (sPE, sPE.n)
                if g + 2 < NGH:
                    h_loads(g + 2)

        hidden_layer(GO, HO, h1t_d, MS_done["h1"], sh1s, g_ready)
        h1_ready = (sACT, sACT.n)
        if ffn_stop == ("h1" if p_ == 0 else "h12"):
            b.wait("sync", h1_ready[0], h1_ready[1])
            break
        hidden_layer(HO, GO, h2t_d, MS_done["h2"], sh2s, h1_ready)
        h2_ready = (sACT, sACT.n)
        if ffn_stop == ("h2" if p_ == 0 else "h22"):
            b.wait("sync", h2_ready[0], h2_ready[1])
            break

        b.wait("gpsimd", dLN2[0])
        b.wait("gpsimd", dLN2[1])
        for q_ in range(3):
            b.wait("gpsimd", dMS[q_], MS_ALL[q_])
        b.wait("gpsimd", sPE)
        w3g_rdy = []
        for gq in range(8):
            b.dma("gpsimd",
                  w3T[:, gq * 4096:(gq + 1) * 4096].rearrange(
                      "p (i o) -> p i o", o=1024),
                  w3t3[:, gq * 4:(gq + 1) * 4, :], dW3g[gq])
            w3g_rdy.append(dW3g[gq].n)

        for sem in dX1:
            b.wait("sync", sem)
        if p_ == 0 and W12_HOIST:
            b.wait("sync", dMX[0])
            b.wait("sync", dMX[1])
            for q_ in range(2):
                b.wait("sync", dMS[q_], LT_MS[q_])
        for tb in range(TFB):
            j2 = tb % 2
            gtb = p_ * TFB + tb
            b.wait("sync", sGPS)
            b.dma("sync", x1t[j2], x1_d[gtb * 128:(gtb + 1) * 128, :], dX1r[j2])
            b.wait("tensor", pf[2][0], pf[2][1])
            b.wait("tensor", h2_ready[0], h2_ready[1])
            for nh in range(2):
                for i in range(FC):
                    if tb == 0 and nh == 0 and i % 4 == 0:
                        b.wait("tensor", dW3g[i // 4], w3g_rdy[i // 4])
                    b.op("tensor", lambda e,
                         o=ps[2][:, nh * 512:(nh + 1) * 512],
                         l=pool[:, GO + i * Tf + tb * 128: GO + i * Tf + tb * 128 + 128],
                         r=w3T[:, i * 1024 + nh * 512: i * 1024 + nh * 512 + 512],
                         st=(i == 0), sp=(i == FC - 1):
                         e.matmul(o, l, r, start=st, stop=sp), inc=sPE)
                    if p_ == 0 and tb == TFB - 1 and nh == 1:
                        w3_last[i] = sPE.n
            w3_done = sPE.n
            b.wait("vector", sPE, w3_done)
            b.wait("vector", sDVE, W3SBC)
            b.wait("vector", sGPS)
            b.op("vector", lambda e: e.tensor_tensor(
                w3tmp, ps[2][:, 0:1024], w3s_bc, AluOpType.mult), inc=sDVE)
            pf[2] = (sDVE, sDVE.n)
            b.wait("gpsimd", sDVE)
            b.wait("gpsimd", dX1r[j2], 16 * (gtb // 2 + 1))
            b.wait("gpsimd", dO, dO.n)
            b.op("gpsimd", lambda e, o=outbuf[j2], x1_=x1t[j2]:
                 e.tensor_add(o, w3tmp, x1_), inc=sGPS)
            b.wait("sync", sGPS)
            b.dma("sync", out_d[gtb * 128:(gtb + 1) * 128, :], outbuf[j2], dO)

    b.wait("sync", dO)
    b.wait("gpsimd", dO)

    with b.stack:
        with nc.Block() as block:
            @block.sync
            def _(e):
                for f in b.q["sync"]:
                    f(e)

            @block.tensor
            def _(e):
                for f in b.q["tensor"]:
                    f(e)

            @block.vector
            def _(e):
                for f in b.q["vector"]:
                    f(e)

            @block.scalar
            def _(e):
                for f in b.q["scalar"]:
                    f(e)

            @block.gpsimd
            def _(e):
                for f in b.q["gpsimd"]:
                    f(e)

    nc.finalize()
    return nc


def s_slot_free_prev_read(sl, s_slot_free):
    return s_slot_free[sl]


def kernel(L0_attn_x, L0_attn_Wq, L0_attn_Wk, L0_attn_Wv, L0_attn_Wo,
           L0_ffn_w1_w, L0_ffn_w1_s, L0_ffn_w2_w, L0_ffn_w2_s,
           L0_ffn_h1_w, L0_ffn_h1_s, L0_ffn_h2_w, L0_ffn_h2_s,
           L0_ffn_w3_w, L0_ffn_w3_s):
    n = 8
    x = np.ascontiguousarray(np.asarray(L0_attn_x, dtype=np.float32))
    shared = {
        "wq": np.ascontiguousarray(np.asarray(L0_attn_Wq, np.float32)),
        "wk": np.ascontiguousarray(np.asarray(L0_attn_Wk, np.float32)),
        "wv": np.ascontiguousarray(np.asarray(L0_attn_Wv, np.float32)),
        "wo": np.ascontiguousarray(np.asarray(L0_attn_Wo, np.float32)),
        "w1": np.ascontiguousarray(np.asarray(L0_ffn_w1_w, np.float32)),
        "w2": np.ascontiguousarray(np.asarray(L0_ffn_w2_w, np.float32)),
        "h1": np.ascontiguousarray(np.asarray(L0_ffn_h1_w, np.float32)),
        "h2": np.ascontiguousarray(np.asarray(L0_ffn_h2_w, np.float32)),
        "w3": np.ascontiguousarray(np.asarray(L0_ffn_w3_w, np.float32)),
        "s1v": np.ascontiguousarray(np.asarray(L0_ffn_w1_s, np.float32)),
        "s2v": np.ascontiguousarray(np.asarray(L0_ffn_w2_s, np.float32)),
        "sh1v": np.ascontiguousarray(np.asarray(L0_ffn_h1_s, np.float32)),
        "sh2v": np.ascontiguousarray(np.asarray(L0_ffn_h2_s, np.float32)),
        "s3v": np.ascontiguousarray(np.asarray(L0_ffn_w3_s, np.float32)),
    }
    nc = build_nc(T=2048)
    in_maps = [dict(shared, x=np.ascontiguousarray(x[i])) for i in range(n)]
    res = run_bass_kernel_spmd(nc, in_maps, core_ids=list(range(n)))
    return np.stack([np.asarray(r["out"]) for r in res.results], axis=0)
